# version 01 (final)
# baseline (speedup 1.0000x reference)
import contextlib
import os
import numpy as np
import concourse.bass as bass
import concourse.mybir as mybir
from concourse.bass_utils import run_bass_kernel_spmd
from concourse.aot_env import allow_ici_all_to_all

F32 = mybir.dt.float32
BF16 = mybir.dt.bfloat16
I8 = mybir.dt.int8
AF = mybir.ActivationFunctionType
ALU = mybir.AluOpType
AXX = mybir.AxisListType.X

D = 1024
NSEQ = 4
N = 4096
DK = 64
DI = 2816
NIC = D // 128
NOC = DI // 128
NKT = N // 512
NQT = N // 512
TOWN = 2048
NQO = TOWN // 512
EXP_BIAS = -12.0


class _Reg:
    __slots__ = ("w", "r")
    def __init__(self):
        self.w = None
        self.r = []


class _Op:
    __slots__ = ("idx", "eng", "sem", "fn", "deps", "dma", "inc", "val", "late")


class Sched:
    NLANE = 4

    def __init__(self):
        self.ops = []
        self.regs = {}
        self._bar = None
        self._bar_done = {}
        self._dma_ct = {}

    def reg(self, *key):
        r = self.regs.get(key)
        if r is None:
            r = _Reg()
            self.regs[key] = r
        return r

    def op(self, eng, fn, reads=(), writes=(), deps=(), dma=False, late=False):
        o = _Op()
        o.idx = len(self.ops)
        o.eng = eng
        if dma:
            ct = self._dma_ct.get(eng, 0)
            self._dma_ct[eng] = ct + 1
            o.sem = f"{eng}D{ct % self.NLANE}"
        else:
            o.sem = eng
        o.fn = fn
        o.dma = dma
        o.late = late
        o.inc = False
        o.val = None
        dd = set()
        if self._bar is not None and not self._bar_done.get(eng):
            dd |= self._bar
            self._bar_done[eng] = True
        for t in deps:
            if t is not None:
                dd.add(t)
        for r in reads:
            if r.w is not None:
                dd.add(r.w)
        for r in writes:
            if r.w is not None:
                dd.add(r.w)
            for t in r.r:
                dd.add(t)
        dd.discard(o)
        o.deps = dd
        self.ops.append(o)
        for r in reads:
            r.r.append(o)
        for r in writes:
            r.w = o
            r.r = []
        return o

    def barrier(self):
        tails = {}
        for o in self.ops:
            tails[o.eng] = o
        self._bar = set(tails.values())
        self._bar_done = {}

    def finalize(self):
        paranoid = bool(int(os.environ.get("K_PARANOID", "0")))
        for o in self.ops:
            if o.dma:
                o.inc = True
            for d in o.deps:
                if paranoid or d.eng != o.eng or d.dma or d.late:
                    d.inc = True
        counters = {}
        for o in self.ops:
            if o.inc:
                counters[o.sem] = counters.get(o.sem, 0) + (16 if o.dma else 1)
                o.val = counters[o.sem]
        self.final_counts = counters

    def body(self, eng_name, sems):
        ops = [o for o in self.ops if o.eng == eng_name]
        def run(eng):
            last = {}
            for o in ops:
                for d in sorted(o.deps, key=lambda d: (d.sem, d.val or 0)):
                    if (d.eng == o.eng and not d.dma and not d.late
                            and not int(os.environ.get("K_PARANOID", "0"))):
                        continue
                    assert d.val is not None, f"dep without sem value {d.eng} {d.idx}"
                    if last.get(d.sem, 0) < d.val:
                        eng.wait_ge(sems[d.sem], d.val)
                        last[d.sem] = d.val
                inst = o.fn(eng)
                if o.inc:
                    inst.then_inc(sems[o.sem], 16 if o.dma else 1)
        return run


def build_program(stage=99, dbg=()):
    nc = bass.Bass("TRN2", target_bir_lowering=False, debug=False)
    sch = Sched()
    R = sch.reg

    xT = nc.dram_tensor("xT", [D, NSEQ * N], F32, kind="ExternalInput").ap()
    xownT = nc.dram_tensor("xownT", [D, TOWN], F32, kind="ExternalInput").ap()
    wqTd = nc.dram_tensor("wqT", [D, 128], F32, kind="ExternalInput").ap()
    wkTd = nc.dram_tensor("wkT", [D, 128], F32, kind="ExternalInput").ap()
    wvTd = nc.dram_tensor("wvT", [D, 128], F32, kind="ExternalInput").ap()
    woTd = nc.dram_tensor("woT", [D, D], F32, kind="ExternalInput").ap()
    wgd = nc.dram_tensor("wg", [DI, D], F32, kind="ExternalInput").ap()
    wud = nc.dram_tensor("wu", [DI, D], F32, kind="ExternalInput").ap()
    wdd = nc.dram_tensor("wd", [D, DI], F32, kind="ExternalInput").ap()
    clip2d = nc.dram_tensor("clip2", [1, 2], F32, kind="ExternalInput").ap()
    identbd = nc.dram_tensor("identb", [128, 128], BF16, kind="ExternalInput").ap()
    identfd = nc.dram_tensor("identf", [128, 128], F32, kind="ExternalInput").ap()
    oneseld = nc.dram_tensor("onesel", [128, 128], F32, kind="ExternalInput").ap()
    masksd = nc.dram_tensor("masks", [128, 4 * 512], BF16, kind="ExternalInput").ap()
    cosdd = nc.dram_tensor("cosd", [128, N], F32, kind="ExternalInput").ap()
    sindd = nc.dram_tensor("sind", [128, N], F32, kind="ExternalInput").ap()
    outT = nc.dram_tensor("outT", [D, TOWN], F32, kind="ExternalOutput").ap()

    cc1i = nc.dram_tensor("cc1i", [1, 512], F32)
    cc1o = nc.dram_tensor("cc1o", [1, 512], F32)
    cc2i = nc.dram_tensor("cc2i", [1, 512], F32)
    cc2o = nc.dram_tensor("cc2o", [1, 512], F32)
    cci = nc.dram_tensor("cci", [8, 128, TOWN], BF16)
    cco = nc.dram_tensor("cco", [8, 128, TOWN], BF16)
    wgs = nc.dram_tensor("wgs", [NOC, NIC, 128, 128], BF16).ap()
    wus = nc.dram_tensor("wus", [NOC, NIC, 128, 128], BF16).ap()
    wds = nc.dram_tensor("wds", [NIC, NOC, 128, 128], BF16).ap()
    hsp = nc.dram_tensor("hsp", [NOC, 128, TOWN], F32).ap()

    dbg_out = {}
    for name, shp, dt in dbg:
        dbg_out[name] = nc.dram_tensor("dbg_" + name, shp, dt, kind="ExternalOutput").ap()

    with contextlib.ExitStack() as ST:
        E = ST.enter_context

        def at(name, shape, dt, off):
            return nc.alloc_sbuf_tensor_at(name, shape, dt, offset=off)

        Z0 = 22016
        ln1 = at("ln1", [128, NIC * TOWN], F32, Z0)
        xq = at("xq", [128, NIC * TOWN], BF16, Z0 + 65536)
        wbuf = [at(f"wbuf{i}", [128, DI], F32, Z0 + i * 11264) for i in range(2)]
        wi8 = at("wi8", [128, DI], I8, Z0 + 22528)
        wbfq = at("wbfq", [128, DI], BF16, Z0 + 25600)
        tstage = at("tstage", [128, DI], BF16, Z0 + 31232)
        attnout = at("attnout", [128, NSEQ * N], BF16, Z0 + 36864)
        KT = at("KT", [128, N], BF16, Z0 + 69632)
        Vb = at("Vb", [128, N], BF16, Z0 + 77824)
        qt_s = [at(f"qt_s{i}", [128, 512], BF16, Z0 + 86016 + i * 1024) for i in range(2)]
        probs = [at(f"probs{i}", [128, 2 * 512], BF16, Z0 + 88064 + i * 2048) for i in range(2)]
        ropea = at("ropea", [128, 512], F32, Z0 + 92160)
        ropeb = at("ropeb", [128, 512], F32, Z0 + 94208)
        bc_s = at("bc_s", [128, 512], F32, Z0 + 96256)
        wo_s = at("wo_s", [128, NIC * 1024], BF16, Z0 + 69632)
        ZB = Z0 + 98304
        attrecv = at("attrecv", [128, NIC * TOWN], BF16, ZB)
        wg_t = [at(f"wg_t{i}", [128, NIC * 128], BF16, ZB + i * 2048) for i in range(2)]
        wu_t = [at(f"wu_t{i}", [128, NIC * 128], BF16, ZB + 4096 + i * 2048) for i in range(2)]
        silu_s = [at(f"silu{i}", [128, 512], F32, ZB + 8192 + i * 2048) for i in range(2)]
        h_s = [at(f"h_s{i}", [128, 512], F32, ZB + 12288 + i * 2048) for i in range(2)]
        hq_qt = at("hq_qt", [128, NOC * 512], BF16, ZB)
        wd_t = [at(f"wd_t{i}", [128, NOC * 128], BF16, ZB + 22528 + i * 5632) for i in range(2)]
        ZC = ZB + 33792
        cos_s = at("cos_s", [128, N], F32, ZC)
        sin_s = at("sin_s", [128, N], F32, ZC + 16384)
        xtile = at("xtile", [128, NIC * 512], F32, ZC)
        sq_s = at("sq_s", [128, 512], F32, ZC + 16384)
        bcr_s = at("bcr_s", [128, 512], F32, ZC + 18432)
        bcmur_s = at("bcmur_s", [128, 512], F32, ZC + 20480)
        i8s = at("i8s", [128, 512], I8, ZC + 22528)
        ZP = ZC + 32768
        cur = [ZP]

        def perm(name, shape, dt):
            h_ = nc.alloc_sbuf_tensor_at(name, shape, dt, offset=cur[0])
            nbytes = int(np.prod(shape[1:])) * np.dtype(mybir.dt.np(dt)).itemsize
            cur[0] += (nbytes + 63) // 64 * 64
            assert cur[0] <= 229344, f"SBUF perm overflow at {name}: {cur[0]}"
            return h_

        cconst = perm("cconst", [128, 4], F32)
        nc.const_aps.register(F32, 0.0, cconst[:, 0:1])
        nc.const_aps.register(F32, 1.0, cconst[:, 1:2])
        nc.const_aps.register(F32, EXP_BIAS, cconst[:, 2:3])
        nc.const_aps.register(F32, 1e-5, cconst[:, 3:4])
        identb_s = perm("identb_s", [128, 128], BF16)
        identf_s = perm("identf_s", [128, 128], F32)
        onesel_s = perm("onesel_s", [128, 128], F32)
        onesb = perm("onesb", [128, 1], BF16)
        onesf = perm("onesf", [128, 1], F32)
        scalesw = perm("scalesw", [128, 64], F32)
        clip_s = perm("clip_s", [1, 2], F32)
        sig_s = perm("sig_s", [1, 2], F32)
        lnrows = perm("lnrows", [128, 1024], F32)
        srows = perm("srows", [128, 1024], F32)
        smalls = perm("smalls", [128, 64], F32)
        gmv = perm("gmv", [128, 2], F32)
        xbf = [perm(f"xbf{i}", [128, NIC * 512], BF16) for i in range(2)]
        masks_s = perm("masks_s", [128, 4 * 512], BF16)
        wqkv_s = perm("wqkv_s", [128, 3 * NIC * 128], BF16)
        SM_RED, SM_T1, SM_IV, SM_ROW2 = 2, 3, 4, 6
        SM_IVX, SM_SCX, SM_IVH, SM_SCH = 8, 9, 10, 11
        SM_SGX, SM_SUX, SM_SDX = 12, 13, 14
        SM_GOT = 16

        ps = [E(nc.psum_tensor(f"ps{i}", [128, 512], F32)) for i in range(7)]
        psb = E(nc.psum_tensor("psb", [128, 128], BF16))

        semnames = ["T", "V", "S", "G"] + [f"{e}D{i}" for e in "YGS" for i in range(Sched.NLANE)]
        sems = {n: E(nc.semaphore(n)) for n in semnames}

        op = sch.op
        rgB, rgXQ = R("rgB"), R("rgXQ")

        _p0list = () if int(os.environ.get("K_NOCONST", "0")) else (
            (identb_s, identbd, None), (identf_s, identfd, None),
            (onesel_s, oneseld, None), (masks_s, masksd, None),
            (clip_s, clip2d, None), (cos_s, cosdd, None), (sin_s, sindd, None))
        for dst, src, rg in _p0list and ((identb_s, identbd, None), (identf_s, identfd, None),
                             (onesel_s, oneseld, None), (masks_s, masksd, None),
                             (clip_s, clip2d, None), (cos_s, cosdd, None), (sin_s, sindd, None)):
            op("Y", (lambda d, s: lambda e: e.dma_start(out=d[:], in_=s))(dst, src),
               writes=[R("buf", id(dst))] + ([rg] if rg else []), dma=True)
        for ci, cv in ((0, 0.0), (1, 1.0), (2, EXP_BIAS), (3, 1e-5)):
            op("V", (lambda c_, v_: lambda e: e.memset(cconst[:, c_:c_ + 1], v_))(ci, cv),
               writes=[R("consts")])
        op("V", lambda e: e.memset(onesb[:], 1.0), writes=[R("buf", id(onesb))])
        op("V", lambda e: e.memset(onesf[:], 1.0), writes=[R("buf", id(onesf))])
        op("V", lambda e: e.memset(lnrows[0:2, :], 0.0), writes=[R("lnrows_z")])
        op("V", lambda e: e.memset(srows[0:2, :], 0.0), writes=[R("srows_h1")])
        op("V", lambda e: e.memset(srows[64:66, :], 0.0), writes=[R("srows_h0")])
        op("V", lambda e: e.memset(smalls[:], 0.0), writes=[R("smalls_z")])
        op("V", lambda e: e.memset(gmv[:], 0.0), writes=[R("gmv")])
        op("S", lambda e: e.activation(out=sig_s[:], in_=clip_s[:], func=AF.Sigmoid),
           reads=[R("buf", id(clip_s)), R("consts")], writes=[R("buf", id(sig_s))])
        if not int(os.environ.get("K_NOWQKV", "0")):
            for j, w in enumerate((wqTd, wkTd, wvTd)):
                src = w.rearrange("(c p) o -> p c o", p=128)
                dst = wqkv_s[:, j * NIC * 128:(j + 1) * NIC * 128].rearrange("p (c o) -> p c o", o=128)
                op("G", (lambda d, s: lambda e: e.dma_start(out=d, in_=s))(dst, src),
                   writes=[R("wqkv", j)], dma=True)

        def wquant(tag, wsrc, nchunk, fwidth, spill, scale_slot):
            ntile = fwidth // 128
            _lim = int(os.environ.get("K_WQN", "99"))
            _part = int(os.environ.get("K_WQPART", "4"))
            for c in range(min(nchunk, _lim)):
                bi = c % 2
                wb = wbuf[bi]
                op("Y", (lambda d, s, fw: lambda e: e.dma_start(
                    out=d[:, 0:fw], in_=s))(wb, wsrc[c * 128:(c + 1) * 128, :], fwidth),
                    writes=[R("wbuf", bi)], dma=True)
                op("V", (lambda wb_, fw: lambda e: e.reduce_max(
                    out=smalls[:, SM_RED:SM_RED + 1], in_=wb_[:, 0:fw], axis=AXX,
                    apply_absolute_value=True))(wb, fwidth),
                    reads=[R("wbuf", bi)], writes=[R("sm_red")], late=True)
                op("V", (lambda slot: lambda e: e.tensor_scalar(
                    out=scalesw[:, slot:slot + 1], in0=smalls[:, SM_RED:SM_RED + 1],
                    scalar1=1.0 / 127.0, scalar2=None, op0=ALU.mult))(scale_slot + c),
                    reads=[R("sm_red")], writes=[R("scw", scale_slot + c)])
                op("V", lambda e: e.reciprocal(out=smalls[:, SM_IV:SM_IV + 1],
                                               in_=smalls[:, SM_RED:SM_RED + 1]),
                   reads=[R("sm_red")], writes=[R("sm_iv")], late=True)
                op("V", lambda e: e.tensor_scalar(
                    out=smalls[:, SM_IV:SM_IV + 1], in0=smalls[:, SM_IV:SM_IV + 1],
                    scalar1=127.0, scalar2=None, op0=ALU.mult),
                   reads=[R("sm_iv")], writes=[R("sm_iv")])
                if _part < 2:
                    continue
                op("S", (lambda wb_, fw: lambda e: e.activation(
                    out=wi8[:, 0:fw], in_=wb_[:, 0:fw], func=AF.Identity,
                    scale=smalls[:, SM_IV:SM_IV + 1]))(wb, fwidth),
                    reads=[R("wbuf", bi), R("sm_iv")], writes=[R("wi8")])
                op("V", (lambda fw: lambda e: e.tensor_copy(
                    out=wbfq[:, 0:fw], in_=wi8[:, 0:fw]))(fwidth),
                    reads=[R("wi8")], writes=[R("wbfq")])
                if _part < 3:
                    continue
                for t in range(ntile):
                    op("T", (lambda t_: lambda e: e.transpose(
                        psb[:], wbfq[:, t_ * 128:(t_ + 1) * 128], identb_s[:]))(t),
                        reads=[R("wbfq"), R("buf", id(identb_s))], writes=[R("psb")])
                    op("V", (lambda t_: lambda e: e.tensor_copy(
                        out=tstage[:, t_ * 128:(t_ + 1) * 128], in_=psb[:]))(t),
                        reads=[R("psb")], writes=[R("tstage", t)])
                if _part < 4:
                    continue
                op("Y", (lambda c_, nt: lambda e: e.dma_start(
                    out=spill[c_].rearrange("c p o -> p c o"),
                    in_=tstage[:, 0:nt * 128].rearrange("p (c o) -> p c o", o=128)))(c, ntile),
                    reads=[R("tstage", t_) for t_ in range(ntile)],
                    writes=[R(tag, c)], dma=True)

        if stage >= 0:
            wquant("WGS", wgd, NOC, D, wgs, 0)
            wquant("WUS", wud, NOC, D, wus, 22)
            wquant("WDS", wdd, NIC, DI, wds, 44)

        pairswap = []
        for i in range(16):
            pairswap += [2 * i + 1, 2 * i]

        def rope_emit(pst, ps_reg, dst_ap, dst_reg, pos_off):
            op("V", (lambda p_: lambda e: e.stream_shuffle(
                out=ropea[:], in_=p_[:], mask=pairswap))(pst),
               reads=[ps_reg], writes=[R("ropea")])
            op("V", (lambda po: lambda e: e.tensor_mul(
                out=ropea[:], in0=ropea[:], in1=sin_s[:, po:po + 512]))(pos_off),
               reads=[R("ropea"), R("buf", id(sin_s))], writes=[R("ropea")])
            op("V", (lambda p_, po: lambda e: e.tensor_mul(
                out=ropeb[:], in0=p_[:], in1=cos_s[:, po:po + 512]))(pst, pos_off),
               reads=[ps_reg, R("buf", id(cos_s))], writes=[R("ropeb")])
            op("V", (lambda d_: lambda e: e.tensor_add(
                out=d_, in0=ropea[:], in1=ropeb[:]))(dst_ap),
               reads=[R("ropea"), R("ropeb")], writes=[dst_reg])

        if stage >= 1:
            for s in range(int(os.environ.get('K_SEQS', str(NSEQ)))):
                for kt in range(NKT):
                    bi = kt % 2
                    xb = xbf[bi]
                    src = xT.rearrange("(c p) n -> p c n", p=128)[:, :, s * N + kt * 512: s * N + kt * 512 + 512]
                    op("G", (lambda d, s_: lambda e: e.dma_start(out=d, in_=s_))(
                        xb[:].rearrange("p (c n) -> p c n", n=512), src),
                        writes=[R("xbf", bi)], dma=True)
                    pk = ps[kt % 2]
                    pkr = R("ps", kt % 2)
                    for ic in range(NIC):
                        op("T", (lambda ic_, xb_, pk_: lambda e: e.matmul(
                            pk_[:], wqkv_s[:, (NIC + ic_) * 128:(NIC + ic_ + 1) * 128],
                            xb_[:, ic_ * 512:(ic_ + 1) * 512],
                            start=(ic_ == 0), stop=(ic_ == NIC - 1)))(ic, xb, pk),
                            reads=[R("wqkv", 1), R("xbf", bi)], writes=[pkr])
                    rope_emit(pk, pkr, KT[:, kt * 512:(kt + 1) * 512], R("KT", kt), kt * 512)
                    for st4 in range(4):
                        kb = kt * 4 + st4
                        pv = ps[2 + st4 % 2]
                        pvr = R("ps", 2 + st4 % 2)
                        for ic in range(NIC):
                            op("T", (lambda ic_, st_, xb_, pv_: lambda e: e.matmul(
                                pv_[0:128, 0:128],
                                xb_[:, ic_ * 512 + st_ * 128: ic_ * 512 + st_ * 128 + 128],
                                wqkv_s[:, (2 * NIC + ic_) * 128:(2 * NIC + ic_ + 1) * 128],
                                start=(ic_ == 0), stop=(ic_ == NIC - 1)))(ic, st4, xb, pv),
                                reads=[R("wqkv", 2), R("xbf", bi)], writes=[pvr])
                        op("V", (lambda kb_, pv_: lambda e: e.tensor_copy(
                            out=Vb[:, kb_ * 128:(kb_ + 1) * 128], in_=pv_[0:128, 0:128]))(kb, pv),
                            reads=[pvr], writes=[R("Vb", kb)])
                for qt in range(NQT):
                    qp = qt % 2
                    xb = xbf[qp]
                    src = xT.rearrange("(c p) n -> p c n", p=128)[:, :, s * N + qt * 512: s * N + qt * 512 + 512]
                    op("G", (lambda d, s_: lambda e: e.dma_start(out=d, in_=s_))(
                        xb[:].rearrange("p (c n) -> p c n", n=512), src),
                        writes=[R("xbf", qp)], dma=True)
                    for ic in range(NIC):
                        op("T", (lambda ic_, xb_: lambda e: e.matmul(
                            ps[6][:], wqkv_s[:, ic_ * 128:(ic_ + 1) * 128],
                            xb_[:, ic_ * 512:(ic_ + 1) * 512],
                            start=(ic_ == 0), stop=(ic_ == NIC - 1)))(ic, xb),
                            reads=[R("wqkv", 0), R("xbf", qp)], writes=[R("ps", 6)])
                    rope_emit(ps[6], R("ps", 6), qt_s[qp][:], R("QT", qp), qt * 512)
                    pa0, pa1 = ps[0], ps[1]
                    nkb = 4 * (qt + 1)
                    for kb in range(nkb):
                        kpar = kb % 2
                        psc0, psc1 = ps[2 + 2 * kpar], ps[3 + 2 * kpar]
                        first, last_ = (kb == 0), (kb == nkb - 1)
                        op("T", (lambda kb_, qp_, p_: lambda e: e.matmul(
                            p_[:], KT[0:64, kb_ * 128:(kb_ + 1) * 128], qt_s[qp_][0:64, :],
                            start=True, stop=True, tile_position=(0, 0)))(kb, qp, psc0),
                            reads=[R("KT", kb // 4), R("QT", qp)], writes=[R("ps", 2 + 2 * kpar)])
                        op("T", (lambda kb_, qp_, p_: lambda e: e.matmul(
                            p_[:], KT[64:128, kb_ * 128:(kb_ + 1) * 128], qt_s[qp_][64:128, :],
                            start=True, stop=True, tile_position=(64, 0)))(kb, qp, psc1),
                            reads=[R("KT", kb // 4), R("QT", qp)], writes=[R("ps", 3 + 2 * kpar)])
                        for hh, psch in ((0, psc0), (1, psc1)):
                            op("S", (lambda h_, p_, kp_: lambda e: e.activation(
                                out=probs[h_][:, kp_ * 512:(kp_ + 1) * 512], in_=p_[:],
                                func=AF.Exp, scale=0.125, bias=EXP_BIAS))(hh, psch, kpar),
                                reads=[R("ps", 2 + hh + 2 * kpar), R("consts")],
                                writes=[R("PR", hh, kpar)])
                        m = kb - 4 * qt
                        if m >= 0:
                            for hh in (0, 1):
                                op("V", (lambda h_, kp_, m_: lambda e: e.tensor_mul(
                                    out=probs[h_][:, kp_ * 512:(kp_ + 1) * 512],
                                    in0=probs[h_][:, kp_ * 512:(kp_ + 1) * 512],
                                    in1=masks_s[:, m_ * 512:(m_ + 1) * 512]))(hh, kpar, m),
                                    reads=[R("PR", hh, kpar), R("buf", id(masks_s))],
                                    writes=[R("PR", hh, kpar)])
                        op("T", (lambda kb_, kp_, f_, l_: lambda e: e.matmul(
                            pa0[0:64, :], Vb[:, kb_ * 128:kb_ * 128 + 64],
                            probs[0][:, kp_ * 512:(kp_ + 1) * 512],
                            start=f_, stop=l_, tile_position=(0, 0)))(kb, kpar, first, last_),
                            reads=[R("PR", 0, kpar), R("Vb", kb)], writes=[R("ps", 0)])
                        op("T", (lambda kb_, kp_, f_, l_: lambda e: e.matmul(
                            pa0[64:65, :], onesb[:],
                            probs[0][:, kp_ * 512:(kp_ + 1) * 512],
                            start=f_, stop=l_, tile_position=(0, 64)))(kb, kpar, first, last_),
                            reads=[R("PR", 0, kpar), R("buf", id(onesb))], writes=[R("ps", 0)])
                        op("T", (lambda kb_, kp_, f_, l_: lambda e: e.matmul(
                            pa1[64:128, :], Vb[:, kb_ * 128 + 64:kb_ * 128 + 128],
                            probs[1][:, kp_ * 512:(kp_ + 1) * 512],
                            start=f_, stop=l_, tile_position=(0, 64)))(kb, kpar, first, last_),
                            reads=[R("PR", 1, kpar), R("Vb", kb)], writes=[R("ps", 1)])
                        op("T", (lambda kb_, kp_, f_, l_: lambda e: e.matmul(
                            pa1[0:1, :], onesb[:],
                            probs[1][:, kp_ * 512:(kp_ + 1) * 512],
                            start=f_, stop=l_, tile_position=(0, 0)))(kb, kpar, first, last_),
                            reads=[R("PR", 1, kpar), R("buf", id(onesb))], writes=[R("ps", 1)])
                    op("V", lambda e: e.reciprocal(out=srows[64:65, 0:512], in_=pa0[64:65, :]),
                       reads=[R("ps", 0)], writes=[R("srows_h0")], late=True)
                    op("V", lambda e: e.reciprocal(out=srows[0:1, 512:1024], in_=pa1[0:1, :]),
                       reads=[R("ps", 1)], writes=[R("srows_h1")], late=True)
                    op("T", lambda e: e.matmul(
                        ps[6][0:64, :], onesel_s[64:66, 0:64], srows[64:66, 0:512],
                        start=True, stop=True, tile_position=(64, 0)),
                       reads=[R("srows_h0"), R("buf", id(onesel_s))], writes=[R("ps", 6)])
                    op("T", lambda e: e.matmul(
                        ps[6][64:128, :], onesel_s[0:2, 64:128], srows[0:2, 512:1024],
                        start=True, stop=True, tile_position=(0, 64)),
                       reads=[R("srows_h1"), R("buf", id(onesel_s))], writes=[R("ps", 6)])
                    op("V", lambda e: e.tensor_copy(out=bc_s[:], in_=ps[6][:]),
                       reads=[R("ps", 6)], writes=[R("bc_s")])
                    col = s * N + qt * 512
                    op("V", (lambda col_: lambda e: e.tensor_mul(
                        out=attnout[0:64, col_:col_ + 512], in0=pa0[0:64, :],
                        in1=bc_s[0:64, :]))(col),
                        reads=[R("ps", 0), R("bc_s")], writes=[R("AO", s, qt)])
                    op("V", (lambda col_: lambda e: e.tensor_mul(
                        out=attnout[64:128, col_:col_ + 512], in0=pa1[64:128, :],
                        in1=bc_s[64:128, :]))(col),
                        reads=[R("ps", 1), R("bc_s")], writes=[R("AO2", s, qt)])

        if stage >= 2:
            for j in range(8):
                sj, hj = j // 2, j % 2
                rr = [R("AO", sj, 4 * hj + k) for k in range(4)] + \
                     [R("AO2", sj, 4 * hj + k) for k in range(4)]
                op("Y", (lambda j_: lambda e: e.dma_start(
                    out=cci.ap()[j_], in_=attnout[:, j_ * TOWN:(j_ + 1) * TOWN]))(j),
                    reads=rr, writes=[R("cci", j)], dma=True)

            def do_a2a(e):
                with allow_ici_all_to_all():
                    return e.collective_compute(
                        "AllToAll", ALU.bypass, replica_groups=[list(range(8))],
                        ins=[cci.ap().opt()], outs=[cco.ap().opt()])
            if int(os.environ.get("K_NOA2A", "0")):
                op("G", lambda e: e.dma_start(out=cco.ap()[:, :, :], in_=cci.ap()[:, :, :]),
                   reads=[R("cci", j) for j in range(8)], writes=[R("cco")], dma=True)
            else:
                op("G", do_a2a, reads=[R("cci", j) for j in range(8)], writes=[R("cco")],
                   late=True)
            for j in range(8):
                op("Y", (lambda j_: lambda e: e.dma_start(
                    out=attrecv[:, j_ * TOWN:(j_ + 1) * TOWN], in_=cco.ap()[j_]))(j),
                    reads=[R("cco")], writes=[R("AR", j), rgB], dma=True)

            sch.barrier()
            op("G", lambda e: e.dma_start(
                out=wo_s[:].rearrange("p (c o) -> p c o", o=1024),
                in_=woTd.rearrange("(c p) o -> p c o", p=128)),
                writes=[R("wo"), rgXQ], dma=True)
            def ln_rows(p1reg, p2reg, psum1, psum2, dmodel=1024.0):
                op("V", (lambda p_: lambda e: e.tensor_scalar(
                    out=lnrows[0:1, 0:512], in0=p_[0:1, :], scalar1=1.0 / dmodel,
                    scalar2=None, op0=ALU.mult))(psum1), reads=[p1reg], writes=[R("mu")], late=True)
                op("V", (lambda p_: lambda e: e.tensor_scalar(
                    out=lnrows[0:1, 512:1024], in0=p_[0:1, :], scalar1=1.0 / dmodel,
                    scalar2=None, op0=ALU.mult))(psum2), reads=[p2reg], writes=[R("es2")], late=True)
                op("V", lambda e: e.scalar_tensor_tensor(
                    out=sq_s[0:1, 0:512], in0=lnrows[0:1, 0:512], scalar=1.0,
                    in1=lnrows[0:1, 0:512], op0=ALU.mult, op1=ALU.mult),
                   reads=[R("mu")], writes=[R("sq_s")], late=True)
                op("V", lambda e: e.tensor_sub(
                    out=lnrows[0:1, 512:1024], in0=lnrows[0:1, 512:1024],
                    in1=sq_s[0:1, 0:512]), reads=[R("es2"), R("sq_s")], writes=[R("es2")], late=True)
                op("S", lambda e: e.activation(
                    out=sq_s[0:1, 0:512], in_=lnrows[0:1, 512:1024], func=AF.Sqrt,
                    bias=1e-5, scale=1.0), reads=[R("es2"), R("consts")], writes=[R("sq_s")])
                op("V", lambda e: e.reciprocal(
                    out=lnrows[0:1, 512:1024], in_=sq_s[0:1, 0:512]),
                   reads=[R("sq_s")], writes=[R("rr")], late=True)
                op("V", lambda e: e.tensor_mul(
                    out=lnrows[0:1, 0:512], in0=lnrows[0:1, 0:512],
                    in1=lnrows[0:1, 512:1024]), reads=[R("mu"), R("rr")], writes=[R("mur")], late=True)
                op("T", lambda e: e.matmul(
                    ps[4][:], onesel_s[0:2, 0:128], lnrows[0:2, 512:1024],
                    start=True, stop=True),
                   reads=[R("rr"), R("lnrows_z"), R("buf", id(onesel_s))], writes=[R("ps", 4)])
                op("T", lambda e: e.matmul(
                    ps[5][:], onesel_s[0:2, 0:128], lnrows[0:2, 0:512],
                    start=True, stop=True),
                   reads=[R("mur"), R("lnrows_z"), R("buf", id(onesel_s))], writes=[R("ps", 5)])
                op("V", lambda e: e.tensor_copy(out=bcr_s[:], in_=ps[4][:]),
                   reads=[R("ps", 4)], writes=[R("bcr_s")])
                op("V", lambda e: e.tensor_copy(out=bcmur_s[:], in_=ps[5][:]),
                   reads=[R("ps", 5)], writes=[R("bcmur_s")])

            for qt in range(NQO):
                xt = xtile
                op("Y", (lambda qt_: lambda e: e.dma_start(
                    out=xtile[:].rearrange("p (c n) -> p c n", n=512),
                    in_=xownT.rearrange("(c p) n -> p c n", p=128)[:, :, qt_ * 512:(qt_ + 1) * 512]))(qt),
                    writes=[R("xtile")], dma=True)
                for oc in range(NIC):
                    py = ps[oc % 2]
                    pyr = R("ps", oc % 2)
                    for dc in range(NIC):
                        op("T", (lambda oc_, dc_, qt_, py_: lambda e: e.matmul(
                            py_[:], wo_s[:, dc_ * 1024 + oc_ * 128: dc_ * 1024 + (oc_ + 1) * 128],
                            attrecv[:, dc_ * TOWN + qt_ * 512: dc_ * TOWN + qt_ * 512 + 512],
                            start=(dc_ == 0), stop=(dc_ == NIC - 1)))(oc, dc, qt, py),
                            reads=[R("wo"), R("AR", dc), rgB, rgXQ], writes=[pyr])
                    lcol = oc * TOWN + qt * 512
                    op("V", (lambda oc_, py_, lc, xt_: lambda e: e.tensor_add(
                        out=ln1[:, lc:lc + 512], in0=py_[:],
                        in1=xt_[:, oc_ * 512:(oc_ + 1) * 512]))(oc, py, lcol, xt),
                        reads=[pyr, R("xtile")], writes=[R("LN1", qt, oc)])
                    op("S", (lambda lc: lambda e: e.activation(
                        out=sq_s[:], in_=ln1[:, lc:lc + 512], func=AF.Square))(lcol),
                        reads=[R("LN1", qt, oc)], writes=[R("sq_s")])
                    op("T", (lambda oc_, lc: lambda e: e.matmul(
                        ps[2][0:1, :], onesf[:], ln1[:, lc:lc + 512],
                        start=(oc_ == 0), stop=(oc_ == NIC - 1)))(oc, lcol),
                        reads=[R("LN1", qt, oc), R("buf", id(onesf))], writes=[R("ps", 2)])
                    op("T", (lambda oc_: lambda e: e.matmul(
                        ps[3][0:1, :], onesf[:], sq_s[:],
                        start=(oc_ == 0), stop=(oc_ == NIC - 1)))(oc),
                        reads=[R("sq_s"), R("buf", id(onesf))], writes=[R("ps", 3)])
                ln_rows(R("ps", 2), R("ps", 3), ps[2], ps[3])
                for oc in range(NIC):
                    lcol = oc * TOWN + qt * 512
                    op("V", (lambda lc: lambda e: e.tensor_mul(
                        out=ln1[:, lc:lc + 512], in0=ln1[:, lc:lc + 512], in1=bcr_s[:]))(lcol),
                        reads=[R("bcr_s"), R("LN1", qt, oc)], writes=[R("LN1", qt, oc)])
                    op("V", (lambda lc: lambda e: e.tensor_sub(
                        out=ln1[:, lc:lc + 512], in0=ln1[:, lc:lc + 512], in1=bcmur_s[:]))(lcol),
                        reads=[R("bcmur_s"), R("LN1", qt, oc)], writes=[R("LN1", qt, oc)])
                    op("V", (lambda lc: lambda e: e.reduce_max(
                        out=smalls[:, SM_RED:SM_RED + 1], in_=ln1[:, lc:lc + 512], axis=AXX,
                        apply_absolute_value=True))(lcol),
                        reads=[R("LN1", qt, oc)], writes=[R("sm_red")], late=True)
                    op("V", lambda e: e.tensor_max(
                        out=gmv[:, 0:1], in0=gmv[:, 0:1], in1=smalls[:, SM_RED:SM_RED + 1]),
                       reads=[R("sm_red")], writes=[R("gmv")])

        def gmax_collective(gcol, cin, cout, sig_col, ivpp):
            _gm = int(os.environ.get("K_GM", "9"))
            if _gm >= 1 and _gm != 13:
                op("T", (lambda g_: lambda e: e.transpose(
                    ps[6][0:1, 0:128], gmv[:, g_:g_ + 1], identf_s[:]))(gcol),
                    reads=[R("gmv"), R("buf", id(identf_s))], writes=[R("ps", 6)])
            if _gm < 2:
                return
            if _gm != 14:
                op("V", lambda e: e.reduce_max(
                    out=smalls[0:1, SM_GOT:SM_GOT + 1], in_=ps[6][0:1, 0:128], axis=AXX),
                   reads=[R("ps", 6)], writes=[R("lmax")], late=True)
            if _gm < 3:
                return
            op("V", (lambda g_: lambda e: e.tensor_copy(
                out=smalls[0:1, 20 + g_:21 + g_], in_=smalls[0:1, SM_GOT:SM_GOT + 1]))(gcol),
               reads=[R("lmax")], writes=[R("dbg_lmax", gcol)], late=True)
            if _gm != 14:
                op("V", lambda e: e.tensor_scalar(
                    out=sq_s[0:1, 0:512], in0=smalls[0:1, SM_GOT:SM_GOT + 1].to_broadcast([1, 512]),
                    scalar1=1.0, scalar2=None, op0=ALU.mult),
                   reads=[R("lmax")], writes=[R("sq_s")], late=True)
            op("G", (lambda ci: lambda e: e.dma_start(
                out=ci.ap()[:, :], in_=sq_s[0:1, 0:512]))(cin),
                reads=[R("sq_s")], writes=[R("ccx", id(cin))], dma=True)
            if int(os.environ.get("K_NOCC", "0")):
                op("G", (lambda ci, co: lambda e: e.dma_start(
                    out=co.ap()[:, :], in_=ci.ap()[:, :]))(cin, cout),
                    reads=[R("ccx", id(cin))], writes=[R("ccx", id(cout))], dma=True)
            else:
                op("G", (lambda ci, co: lambda e: e.collective_compute(
                    "AllReduce", ALU.max, replica_groups=[list(range(8))],
                    ins=[ci.ap().opt()], outs=[co.ap().opt()]))(cin, cout),
                    reads=[R("ccx", id(cin))], writes=[R("ccx", id(cout))], late=True)
            op("G", (lambda co: lambda e: e.dma_start(
                out=smalls[0:1, SM_GOT:SM_GOT + 1], in_=co.ap()[:, 0:1]))(cout),
                reads=[R("ccx", id(cout)), R("lmax")], writes=[R("gmax_g")], dma=True)
            if _gm < 4:
                return
            op("V", (lambda g_: lambda e: e.tensor_copy(
                out=smalls[0:1, 24 + g_:25 + g_], in_=smalls[0:1, SM_GOT:SM_GOT + 1]))(gcol),
               reads=[R("gmax_g")], writes=[R("dbg_gmax", gcol)])
            op("V", (lambda sc_: lambda e: e.tensor_mul(
                out=smalls[0:1, SM_T1:SM_T1 + 1], in0=sig_s[0:1, sc_:sc_ + 1],
                in1=smalls[0:1, SM_GOT:SM_GOT + 1]))(sig_col),
                reads=[R("gmax_g"), R("buf", id(sig_s))], writes=[R("sm_t1")], late=True)
            op("V", lambda e: e.tensor_scalar(
                out=smalls[0:1, SM_ROW2 + 1:SM_ROW2 + 2],
                in0=smalls[0:1, SM_T1:SM_T1 + 1], scalar1=1.0 / 127.0, scalar2=None,
                op0=ALU.mult), reads=[R("sm_t1")], writes=[R("sm_scv")], late=True)
            op("V", lambda e: e.reciprocal(
                out=smalls[0:1, SM_ROW2:SM_ROW2 + 1],
                in_=smalls[0:1, SM_ROW2 + 1:SM_ROW2 + 2]),
               reads=[R("sm_scv")], writes=[R("sm_ivv")], late=True)
            op("V", (lambda g_: lambda e: e.tensor_copy(
                out=smalls[0:1, 26 + 4 * g_:28 + 4 * g_], in_=sig_s[0:1, 0:2]))(gcol),
               reads=[R("buf", id(sig_s))], writes=[R("dbg_sig", gcol)])
            op("V", (lambda g_: lambda e: e.tensor_copy(
                out=smalls[0:1, 34 + 2 * g_:36 + 2 * g_],
                in_=smalls[0:1, SM_ROW2:SM_ROW2 + 2]))(gcol),
               reads=[R("sm_ivv"), R("sm_scv")], writes=[R("dbg_ivsc", gcol)], late=True)
            op("T", lambda e: e.matmul(
                ps[6][0:128, 0:2], onesel_s[0:2, 0:128],
                smalls[0:2, SM_ROW2:SM_ROW2 + 2], start=True, stop=True),
               reads=[R("sm_ivv"), R("sm_scv"), R("smalls_z"), R("buf", id(onesel_s))],
               writes=[R("ps", 6)])
            op("V", (lambda a: lambda e: e.tensor_copy(
                out=smalls[:, a:a + 2], in_=ps[6][0:128, 0:2]))(ivpp),
                reads=[R("ps", 6)], writes=[R("sm_pp", ivpp)], late=True)

        if int(os.environ.get("K_ARONLY", "0")):
            gmax_collective(0, cc1i, cc1o, 0, SM_IVX)
        if stage >= 3:
            gmax_collective(0, cc1i, cc1o, 0, SM_IVX)
            for qt in range(NQO):
                for oc in range(NIC):
                    lcol = oc * TOWN + qt * 512
                    op("S", (lambda lc: lambda e: e.activation(
                        out=i8s[:], in_=ln1[:, lc:lc + 512], func=AF.Identity,
                        scale=smalls[:, SM_IVX:SM_IVX + 1]))(lcol),
                        reads=[R("LN1", qt, oc), R("sm_pp", SM_IVX)], writes=[R("i8s")])
                    op("V", (lambda lc: lambda e: e.tensor_copy(
                        out=xq[:, lc:lc + 512], in_=i8s[:]))(lcol),
                        reads=[R("i8s")], writes=[R("XQ", qt, oc), rgXQ])

        if stage >= 3.5:
            for oc in range(NOC):
                wp = oc % 2
                op("Y", (lambda oc_, w_: lambda e: e.dma_start(
                    out=w_[:].rearrange("p (c o) -> p c o", o=128), in_=wgs[oc_].rearrange("c p o -> p c o")))(oc, wg_t[wp]),
                    writes=[R("wgt", wp), rgB], deps=[R("WGS", oc).w], dma=True)
                op("Y", (lambda oc_, w_: lambda e: e.dma_start(
                    out=w_[:].rearrange("p (c o) -> p c o", o=128), in_=wus[oc_].rearrange("c p o -> p c o")))(oc, wu_t[wp]),
                    writes=[R("wut", wp), rgB], deps=[R("WUS", oc).w], dma=True)
                op("V", (lambda oc_: lambda e: e.tensor_mul(
                    out=smalls[:, SM_SGX:SM_SGX + 1], in0=scalesw[:, oc_:oc_ + 1],
                    in1=smalls[:, SM_SCX:SM_SCX + 1]))(oc),
                    reads=[R("scw", oc), R("sm_pp", SM_IVX)], writes=[R("sm_sgx")], late=True)
                op("V", (lambda oc_: lambda e: e.tensor_mul(
                    out=smalls[:, SM_SUX:SM_SUX + 1], in0=scalesw[:, 22 + oc_:23 + oc_],
                    in1=smalls[:, SM_SCX:SM_SCX + 1]))(oc),
                    reads=[R("scw", 22 + oc), R("sm_pp", SM_IVX)], writes=[R("sm_sux")], late=True)
                for qt in range(NQO):
                    pg, pu = ps[qt % 2], ps[2 + qt % 2]
                    pgr, pur = R("ps", qt % 2), R("ps", 2 + qt % 2)
                    for ic in range(NIC):
                        op("T", (lambda ic_, qt_, w_, p_: lambda e: e.matmul(
                            p_[:], w_[:, ic_ * 128:(ic_ + 1) * 128],
                            xq[:, ic_ * TOWN + qt_ * 512: ic_ * TOWN + qt_ * 512 + 512],
                            start=(ic_ == 0), stop=(ic_ == NIC - 1)))(ic, qt, wg_t[wp], pg),
                            reads=[R("wgt", wp), R("XQ", qt, ic)], writes=[pgr])
                    for ic in range(NIC):
                        op("T", (lambda ic_, qt_, w_, p_: lambda e: e.matmul(
                            p_[:], w_[:, ic_ * 128:(ic_ + 1) * 128],
                            xq[:, ic_ * TOWN + qt_ * 512: ic_ * TOWN + qt_ * 512 + 512],
                            start=(ic_ == 0), stop=(ic_ == NIC - 1)))(ic, qt, wu_t[wp], pu),
                            reads=[R("wut", wp), R("XQ", qt, ic)], writes=[pur])
                    sl = silu_s[qt % 2]
                    hs = h_s[qt % 2]
                    op("S", (lambda p_, s_: lambda e: e.activation(
                        out=s_[:], in_=p_[:], func=AF.Silu,
                        scale=smalls[:, SM_SGX:SM_SGX + 1]))(pg, sl),
                        reads=[pgr, R("sm_sgx")], writes=[R("silu", qt % 2), rgB])
                    op("V", (lambda p_, s_, h_: lambda e: e.scalar_tensor_tensor(
                        out=h_[:], in0=p_[:], scalar=smalls[:, SM_SUX:SM_SUX + 1],
                        in1=s_[:], op0=ALU.mult, op1=ALU.mult))(pu, sl, hs),
                        reads=[pur, R("silu", qt % 2), R("sm_sux")],
                        writes=[R("h_s", qt % 2), rgB])
                    op("V", (lambda h_: lambda e: e.reduce_max(
                        out=smalls[:, SM_RED:SM_RED + 1], in_=h_[:], axis=AXX,
                        apply_absolute_value=True))(hs),
                        reads=[R("h_s", qt % 2)], writes=[R("sm_red")], late=True)
                    op("V", lambda e: e.tensor_max(
                        out=gmv[:, 1:2], in0=gmv[:, 1:2], in1=smalls[:, SM_RED:SM_RED + 1]),
                       reads=[R("sm_red")], writes=[R("gmv")])
                    op("Y", (lambda oc_, qt_, h_: lambda e: e.dma_start(
                        out=hsp[oc_, :, qt_ * 512:(qt_ + 1) * 512], in_=h_[:]))(oc, qt, hs),
                        reads=[R("h_s", qt % 2)], writes=[R("HSP", oc, qt)], dma=True)

        if stage >= 4:
            gmax_collective(1, cc2i, cc2o, 1, SM_IVH)
            for qt in range(NQO):
                for icc in range(NOC):
                    hp_ = icc % 2
                    xsl = xtile[:, hp_ * 512:(hp_ + 1) * 512]
                    hdeps = ([R("xtile").w] + list(R("xtile").r)) if (qt == 0 and icc < 2) else []
                    op("Y", (lambda icc_, qt_, x_: lambda e: e.dma_start(
                        out=x_, in_=hsp[icc_, :, qt_ * 512:(qt_ + 1) * 512]))(icc, qt, xsl),
                        reads=[R("HSP", icc, qt)], writes=[R("xtileh", hp_)],
                        deps=hdeps, dma=True)
                    op("S", (lambda x_: lambda e: e.activation(
                        out=i8s[:], in_=x_, func=AF.Identity,
                        scale=smalls[:, SM_IVH:SM_IVH + 1]))(xsl),
                        reads=[R("xtileh", hp_), R("sm_pp", SM_IVH)], writes=[R("i8s")])
                    op("V", (lambda icc_: lambda e: e.tensor_copy(
                        out=hq_qt[:, icc_ * 512:(icc_ + 1) * 512], in_=i8s[:]))(icc),
                        reads=[R("i8s")], writes=[R("HQ", qt, icc), rgB])
                for oc in range(NIC):
                    wp = oc % 2
                    op("Y", (lambda oc_, w_: lambda e: e.dma_start(
                        out=w_[:].rearrange("p (c o) -> p c o", o=128), in_=wds[oc_].rearrange("c p o -> p c o")))(oc, wd_t[wp]),
                        writes=[R("wdt", wp), rgB],
                        deps=[R("WDS", oc).w], dma=True)
                    op("V", (lambda oc_: lambda e: e.tensor_mul(
                        out=smalls[:, SM_SDX:SM_SDX + 1], in0=scalesw[:, 44 + oc_:45 + oc_],
                        in1=smalls[:, SM_SCH:SM_SCH + 1]))(oc),
                        reads=[R("scw", 44 + oc), R("sm_pp", SM_IVH)], writes=[R("sm_sdx")], late=True)
                    pd = ps[4 + oc % 2]
                    pdr = R("ps", 4 + oc % 2)
                    for icc in range(NOC):
                        op("T", (lambda icc_, w_, p_: lambda e: e.matmul(
                            p_[:], w_[:, icc_ * 128:(icc_ + 1) * 128],
                            hq_qt[:, icc_ * 512:(icc_ + 1) * 512],
                            start=(icc_ == 0), stop=(icc_ == NOC - 1)))(icc, wd_t[wp], pd),
                            reads=[R("wdt", wp), R("HQ", qt, icc)], writes=[pdr])
                    lcol = oc * TOWN + qt * 512
                    op("V", (lambda p_, lc: lambda e: e.scalar_tensor_tensor(
                        out=ln1[:, lc:lc + 512], in0=p_[:],
                        scalar=smalls[:, SM_SDX:SM_SDX + 1], in1=ln1[:, lc:lc + 512],
                        op0=ALU.mult, op1=ALU.add))(pd, lcol),
                        reads=[pdr, R("sm_sdx"), R("LN1", qt, oc)],
                        writes=[R("LN1", qt, oc)])
                    op("S", (lambda lc: lambda e: e.activation(
                        out=sq_s[:], in_=ln1[:, lc:lc + 512], func=AF.Square))(lcol),
                        reads=[R("LN1", qt, oc)], writes=[R("sq_s")])
                    op("T", (lambda oc_, lc: lambda e: e.matmul(
                        ps[2][0:1, :], onesf[:], ln1[:, lc:lc + 512],
                        start=(oc_ == 0), stop=(oc_ == NIC - 1)))(oc, lcol),
                        reads=[R("LN1", qt, oc), R("buf", id(onesf))], writes=[R("ps", 2)])
                    op("T", (lambda oc_: lambda e: e.matmul(
                        ps[3][0:1, :], onesf[:], sq_s[:],
                        start=(oc_ == 0), stop=(oc_ == NIC - 1)))(oc),
                        reads=[R("sq_s"), R("buf", id(onesf))], writes=[R("ps", 3)])
                ln_rows(R("ps", 2), R("ps", 3), ps[2], ps[3])
                for oc in range(NIC):
                    lcol = oc * TOWN + qt * 512
                    op("V", (lambda lc: lambda e: e.tensor_mul(
                        out=ln1[:, lc:lc + 512], in0=ln1[:, lc:lc + 512], in1=bcr_s[:]))(lcol),
                        reads=[R("bcr_s"), R("LN1", qt, oc)], writes=[R("LN1", qt, oc)])
                    op("V", (lambda lc: lambda e: e.tensor_sub(
                        out=ln1[:, lc:lc + 512], in0=ln1[:, lc:lc + 512], in1=bcmur_s[:]))(lcol),
                        reads=[R("bcmur_s"), R("LN1", qt, oc)], writes=[R("LN1", qt, oc)])
                    o_ = op("Y", (lambda oc_, qt_, lc: lambda e: e.dma_start(
                        out=outT[oc_ * 128:(oc_ + 1) * 128, qt_ * 512:(qt_ + 1) * 512],
                        in_=ln1[:, lc:lc + 512]))(oc, qt, lcol),
                        reads=[R("LN1", qt, oc)], writes=[R("OUT", qt, oc)], dma=True)
                    o_.inc = True

        for name, shp, dt in dbg:
            if name == "KT":
                op("Y", lambda e: e.dma_start(out=dbg_out["KT"], in_=KT[:]),
                   reads=[R("KT", k) for k in range(NKT)], dma=True).inc = True
            elif name == "Vb":
                op("Y", lambda e: e.dma_start(out=dbg_out["Vb"], in_=Vb[:]),
                   reads=[R("Vb", k) for k in range(32)], dma=True).inc = True
            elif name == "AO":
                op("Y", lambda e: e.dma_start(out=dbg_out["AO"], in_=attnout[:]),
                   reads=[R("AO", s_, q_) for s_ in range(4) for q_ in range(8)] +
                         [R("AO2", s_, q_) for s_ in range(4) for q_ in range(8)],
                   dma=True).inc = True
            elif name == "LN1":
                op("Y", lambda e: e.dma_start(out=dbg_out["LN1"], in_=ln1[:]),
                   reads=[R("LN1", q_, o2) for q_ in range(4) for o2 in range(8)],
                   dma=True).inc = True
            elif name == "XQ":
                op("Y", lambda e: e.dma_start(out=dbg_out["XQ"], in_=xq[:]),
                   reads=[R("XQ", q_, o2) for q_ in range(4) for o2 in range(8)],
                   dma=True).inc = True
            elif name == "SM":
                op("Y", lambda e: e.dma_start(out=dbg_out["SM"], in_=smalls[:]),
                   reads=[R("sm_pp", SM_IVX), R("dbg_lmax", 0), R("dbg_gmax", 0), R("dbg_sig", 0), R("dbg_ivsc", 0)],
                   dma=True).inc = True
            elif name == "SCW":
                op("Y", lambda e: e.dma_start(out=dbg_out["SCW"], in_=scalesw[:]),
                   reads=[R("scw", k) for k in range(52)], dma=True).inc = True

        sch.finalize()
        bodies = {n: sch.body(n, sems) for n in ("T", "V", "S", "G", "Y")}

        with nc.Block() as block:
            block.tensor(bodies["T"])
            block.vector(bodies["V"])
            block.scalar(bodies["S"])
            block.gpsimd(bodies["G"])

            def ybody(eng):
                bodies["Y"](eng)
                for sname, cnt in sorted(sch.final_counts.items()):
                    if "D" in sname and cnt:
                        eng.wait_ge(sems[sname], cnt)
            block.sync(ybody)
    return nc


def _stage_inputs(inputs):
    import ml_dtypes
    x = np.ascontiguousarray(np.asarray(inputs["L0_attn_x"], dtype=np.float32))
    Wq = np.asarray(inputs["L0_attn_Wq"], dtype=np.float32)
    Wk = np.asarray(inputs["L0_attn_Wk"], dtype=np.float32)
    Wv = np.asarray(inputs["L0_attn_Wv"], dtype=np.float32)
    Wo = np.asarray(inputs["L0_attn_Wo"], dtype=np.float32)
    wg = np.ascontiguousarray(np.asarray(inputs["L0_ffn_w_gate"], dtype=np.float32))
    wu = np.ascontiguousarray(np.asarray(inputs["L0_ffn_w_up"], dtype=np.float32))
    wdn = np.ascontiguousarray(np.asarray(inputs["L0_ffn_w_down"], dtype=np.float32))
    clipx = float(np.asarray(inputs["L0_ffn_clip_x"]).reshape(-1)[0])
    clipd = float(np.asarray(inputs["L0_ffn_clip_down"]).reshape(-1)[0])

    xT = np.ascontiguousarray(x.transpose(2, 0, 1).reshape(D, NSEQ * N))
    wqT = np.ascontiguousarray(Wq.T)
    wkT = np.ascontiguousarray(Wk.T)
    wvT = np.ascontiguousarray(Wv.T)
    woT = np.ascontiguousarray(Wo.T)
    clip2 = np.array([[clipx, clipd]], dtype=np.float32)

    identb = np.eye(128, dtype=ml_dtypes.bfloat16)
    identf = np.eye(128, dtype=np.float32)
    onesel = np.zeros((128, 128), dtype=np.float32)
    onesel[0, :] = 1.0
    onesel[64, :] = 1.0
    masks = np.zeros((128, 4 * 512), dtype=np.float32)
    kk = np.arange(128)[:, None]
    qq = np.arange(512)[None, :]
    for m in range(4):
        masks[:, m * 512:(m + 1) * 512] = (kk + 128 * m <= qq)
    masks = masks.astype(ml_dtypes.bfloat16)
    p = np.arange(128)
    j = (p % 64) // 2
    inv = 10000.0 ** (-(2.0 * j) / DK)
    t = np.arange(N, dtype=np.float64)
    ang = t[None, :] * inv[:, None]
    cosd = np.cos(ang).astype(np.float32)
    sgn = np.where(p % 2 == 0, -1.0, 1.0)
    sind = (np.sin(ang) * sgn[:, None]).astype(np.float32)

    maps = []
    for c in range(8):
        s, h = c // 2, c % 2
        xownT = np.ascontiguousarray(xT[:, s * N + h * TOWN: s * N + (h + 1) * TOWN])
        maps.append({
            "xT": xT, "xownT": xownT,
            "wqT": np.ascontiguousarray(wqT[:, c * 128:(c + 1) * 128]),
            "wkT": np.ascontiguousarray(wkT[:, c * 128:(c + 1) * 128]),
            "wvT": np.ascontiguousarray(wvT[:, c * 128:(c + 1) * 128]),
            "woT": woT, "wg": wg, "wu": wu, "wd": wdn, "clip2": clip2,
            "identb": identb, "identf": identf, "onesel": onesel,
            "masks": masks, "cosd": cosd, "sind": sind,
        })
    return maps


def _assemble(results):
    out = np.empty((NSEQ, N, D), dtype=np.float32)
    for c in range(8):
        s, h = c // 2, c % 2
        out[s, h * TOWN:(h + 1) * TOWN, :] = results[c]["outT"].T
    return out


def kernel(**inputs):
    maps = _stage_inputs(inputs)
    nc = build_program()
    res = run_bass_kernel_spmd(nc, maps, core_ids=list(range(8)))
    return _assemble(res.results)
